# version 01 (final)
# baseline (speedup 1.0000x reference)
import numpy as np
from contextlib import ExitStack

import concourse.bass as bass
import concourse.mybir as mybir
from concourse.bass_utils import run_bass_kernel_spmd

F32 = mybir.dt.float32
BF16 = mybir.dt.bfloat16
AF = mybir.ActivationFunctionType
ALU = mybir.AluOpType
AX = mybir.AxisListType

B, S, D = 2, 4096, 2048
H, DK = 16, 128
E, FF = 8, 1024
NCORE = 8
NOWN = 1024
NDC = D // 128
NTAU = S // 512
NKB = S // 128
NM = 8
NFC = FF // 128
THETA = 10000.0
ISQ = 1.0 / np.sqrt(DK)
NEG = -1.0e30
SHIFT = 24.0
EPS = 1e-5


def build_nc(phase_limit=99, taps=(), p2_pairs=8, p2_stages='ABC'):
    nc = bass.Bass("TRN2", target_bir_lowering=False)
    taps = set(taps)

    def din(name, shape, dt=F32):
        return nc.dram_tensor(name, shape, dt, kind="ExternalInput").ap()

    xcov = din("xcov", [D, S])
    x_own = din("x_own", [D, NOWN])
    ropeq = din("ropeq", [128, NOWN])
    ropek = din("ropek", [128, S])
    maskS = din("maskS", [128, 512])
    maskT = din("maskT", [128, 512])
    i128 = din("i128", [128, 128])
    ones1 = din("ones1", [1, 128])
    onesc = din("onesc", [128, 1])
    sel = din("sel", [8, 1024])
    wqT = din("wqT", [D, D])
    wkT = din("wkT", [D, D])
    wvT = din("wvT", [D, D])
    woT = din("woT", [D, D])
    rwT = din("rwT", [D, 8])
    w1 = din("w1", [E, D, FF])
    w2 = din("w2", [E, D, FF])
    w3 = din("w3", [E, FF, D])

    yout = nc.dram_tensor("yout", [D, NOWN], F32, kind="ExternalOutput").ap()
    tap_out = {}

    def tapt(name, shape, dt=F32):
        if name in taps:
            tap_out[name] = nc.dram_tensor("tap_" + name, shape, dt,
                                           kind="ExternalOutput").ap()
        return tap_out.get(name)

    t_q = tapt("q", [128, H, NOWN], BF16)
    t_k = tapt("k", [128, NDC, S], BF16)
    t_v = tapt("v", [128, NKB, D], BF16)
    t_ctx = tapt("ctx", [128, H, NOWN], BF16)
    t_y1 = tapt("y1", [128, NDC, NOWN])
    t_w = tapt("w", [128, NM, 8])

    k_all = nc.dram_tensor("k_all", [D, S], BF16, kind="Internal").ap()
    v_all = nc.dram_tensor("v_all", [S, D], BF16, kind="Internal").ap()

    stack = ExitStack()
    with stack:
        t = nc.tensor
        v = nc.vector
        s = nc.scalar
        g = nc.gpsimd
        sy = nc.sync
        ENGS = (t, v, s, g, sy)

        class Sem:
            def __init__(self, name):
                self.h = stack.enter_context(nc.semaphore(name))
                self.n = 0

            def inc(self, instr, k=1):
                instr.then_inc(self.h, k)
                self.n += k
                return self.n

            def dma(self, instr):
                return self.inc(instr, 16)

        sem_g = Sem("ldg")
        sem_s = Sem("lds")
        sem_a = Sem("lda")
        sem_pe = Sem("pe")
        sem_ve = Sem("ve")
        sem_sc = Sem("sc")
        sem_po = Sem("po")
        sem_st = Sem("st")
        sem_bar = Sem("bar")

        def barrier():
            for e in ENGS:
                e.sem_inc(sem_bar.h, 1)
                sem_bar.n += 1
            for e in ENGS:
                e.wait_ge(sem_bar.h, sem_bar.n)

        def sbr(name, shape, dt=F32):
            return stack.enter_context(nc.sbuf_tensor(name, shape, dt, side="right"))

        i128_s = sbr("i128_s", [128, 128])
        ones1_s = sbr("ones1_s", [1, 128])
        onesc_b = sbr("onesc_b", [128, 1], BF16)
        sel_s = sbr("sel_s", [8, 1024])
        row_a = sbr("row_a", [1, 512])
        row_b = sbr("row_b", [1, 512])
        row_c = sbr("row_c", [1, 512])
        ab_t = sbr("ab_t", [128, 2, 512])

        sem_s.dma(sy.dma_start(i128_s[:, :], i128))
        sem_s.dma(sy.dma_start(ones1_s[:, :], ones1))
        sem_s.dma(sy.dma_start(sel_s[:, :], sel))
        sem_g.dma(g.dma_start(onesc_b[:, :], onesc))
        for e in ENGS:
            e.wait_ge(sem_s.h, sem_s.n)
            e.wait_ge(sem_g.h, sem_g.n)

        def emit_ln(sf, lnb, lnq, psA, psB, psBC):
            for ti in range(2):
                pe_base = [0]
                for dc in range(NDC):
                    bsl = lnb[:, dc % 2, :]
                    qsl = lnq[:, dc % 2, :]
                    if dc < 2:
                        g.wait_ge(sem_pe.h, sem_pe.n)
                        s.wait_ge(sem_pe.h, sem_pe.n)
                    else:
                        g.wait_ge(sem_pe.h, pe_base[0] + dc - 1)
                        s.wait_ge(sem_pe.h, pe_base[0] + dc - 1)
                    sem_po.inc(g.tensor_copy(bsl, sf[:, dc, ti * 512:(ti + 1) * 512]))
                    s.wait_ge(sem_po.h, sem_po.n)
                    sem_sc.inc(s.activation(qsl, bsl, AF.Square))
                    t.wait_ge(sem_sc.h, sem_sc.n)
                    if dc == 0:
                        pe_base[0] = sem_pe.n
                    t.matmul(psA[0:1, :], onesc_b[:, :], bsl,
                             start=(dc == 0), stop=(dc == NDC - 1))
                    sem_pe.inc(t.matmul(psB[0:1, :], onesc_b[:, :], qsl,
                                        start=(dc == 0), stop=(dc == NDC - 1)))
                v.wait_ge(sem_pe.h, sem_pe.n)
                v.tensor_scalar_mul(row_a[:, :], psA[0:1, :], 1.0 / D)
                v.drain()
                v.tensor_tensor(row_c[:, :], row_a[:, :], row_a[:, :], ALU.mult)
                v.drain()
                v.tensor_scalar(row_b[:, :], psB[0:1, :], 1.0 / D, EPS,
                                ALU.mult, ALU.add)
                v.drain()
                sem_ve.inc(v.tensor_tensor(row_b[:, :], row_b[:, :], row_c[:, :],
                                           ALU.subtract))
                s.wait_ge(sem_ve.h, sem_ve.n)
                sem_sc.inc(s.activation(row_b[:, :], row_b[:, :], AF.Ln))
                v.wait_ge(sem_sc.h, sem_sc.n)
                sem_ve.inc(v.tensor_scalar_mul(row_b[:, :], row_b[:, :], -0.5))
                s.wait_ge(sem_ve.h, sem_ve.n)
                sem_sc.inc(s.activation(row_b[:, :], row_b[:, :], AF.Exp))
                v.wait_ge(sem_sc.h, sem_sc.n)
                v.tensor_tensor(row_c[:, :], row_a[:, :], row_b[:, :], ALU.mult)
                v.drain()
                sem_ve.inc(v.tensor_scalar_mul(row_c[:, :], row_c[:, :], -1.0))
                t.wait_ge(sem_ve.h, sem_ve.n)
                sem_pe.inc(t.matmul(psBC[:, 0:512], ones1_s[:, :], row_b[:, :],
                                    start=True, stop=True))
                s.wait_ge(sem_pe.h, sem_pe.n)
                sem_sc.inc(s.copy(ab_t[:, 0, :], psBC[:, 0:512]))
                t.wait_ge(sem_sc.h, sem_sc.n)
                sem_pe.inc(t.matmul(psBC[:, 0:512], ones1_s[:, :], row_c[:, :],
                                    start=True, stop=True))
                s.wait_ge(sem_pe.h, sem_pe.n)
                sem_sc.inc(s.copy(ab_t[:, 1, :], psBC[:, 0:512]))
                v.wait_ge(sem_sc.h, sem_sc.n)
                g.wait_ge(sem_sc.h, sem_sc.n)
                for dc in range(NDC):
                    eng, sm = (v, sem_ve) if dc % 2 == 0 else (g, sem_po)
                    sl = sf[:, dc, ti * 512:(ti + 1) * 512]
                    eng.tensor_tensor(sl, sl, ab_t[:, 0, :], ALU.mult)
                    ii = eng.tensor_tensor(sl, sl, ab_t[:, 1, :], ALU.add)
                    if dc >= NDC - 2:
                        sm.inc(ii)
                for e in (t, v, s, g):
                    e.wait_ge(sem_ve.h, sem_ve.n)
                    e.wait_ge(sem_po.h, sem_po.n)

        ctx_g = nc.sbuf_tensor("ctx_bf", [128, H, NOWN], BF16)
        ctx_bf = ctx_g.__enter__()
        q_g = nc.sbuf_tensor("q_bf", [128, H, NOWN], BF16)
        q_bf = q_g.__enter__()
        if True:
         if True:

            with nc.sbuf_tensor("wq_bf", [128, NDC, D], BF16) as wq_bf, \
                 nc.sbuf_tensor("xo_bf", [128, NDC, NOWN], BF16) as xo_bf, \
                 nc.sbuf_tensor("ropeq_s", [128, NOWN], F32) as ropeq_s, \
                 nc.sbuf_tensor("r1", [64, 512], F32) as r1, \
                 nc.sbuf_tensor("r2", [64, 512], F32) as r2, \
                 nc.psum_tensor("psq0", [128, 512], F32) as psq0, \
                 nc.psum_tensor("psq1", [128, 512], F32) as psq1:
                psq = (psq0, psq1)
                sem_s.dma(sy.dma_start(ropeq_s[:, :], ropeq))
                wqr = wqT.rearrange("(c p) o -> p c o", p=128)
                for i in range(4):
                    sem_g.dma(g.dma_start(wq_bf[:, 4 * i:4 * (i + 1), :],
                                          wqr[:, 4 * i:4 * (i + 1), :]))
                xo_r = x_own.rearrange("(c p) n -> p c n", p=128)
                for i in range(2):
                    sem_g.dma(g.dma_start(xo_bf[:, 8 * i:8 * (i + 1), :],
                                          xo_r[:, 8 * i:8 * (i + 1), :]))
                t.wait_ge(sem_g.h, sem_g.n)
                v.wait_ge(sem_s.h, sem_s.n)
                rope_done = {}
                nit = 0
                for h in range(H):
                    for ti in range(2):
                        bank = psq[nit % 2]
                        if nit >= 2:
                            t.wait_ge(sem_ve.h, rope_done[nit - 2])
                        for dc in range(NDC):
                            mm = t.matmul(bank[:, :],
                                          wq_bf[:, dc, h * 128:(h + 1) * 128],
                                          xo_bf[:, dc, ti * 512:(ti + 1) * 512],
                                          start=(dc == 0), stop=(dc == NDC - 1))
                        sem_pe.inc(mm)
                        v.wait_ge(sem_pe.h, sem_pe.n)
                        co = ropeq_s[0:64, ti * 512:(ti + 1) * 512]
                        si = ropeq_s[64:128, ti * 512:(ti + 1) * 512]
                        v.tensor_tensor(r1[:, :], bank[0:64, :], co, ALU.mult)
                        v.drain()
                        v.tensor_tensor(r2[:, :], bank[64:128, :], si, ALU.mult)
                        v.drain()
                        sem_ve.inc(v.tensor_tensor(
                            q_bf[0:64, h, ti * 512:(ti + 1) * 512],
                            r1[:, :], r2[:, :], ALU.subtract))
                        v.drain()
                        v.tensor_tensor(r1[:, :], bank[0:64, :], si, ALU.mult)
                        v.drain()
                        v.tensor_tensor(r2[:, :], bank[64:128, :], co, ALU.mult)
                        v.drain()
                        rope_done[nit] = sem_ve.inc(v.tensor_tensor(
                            q_bf[64:128, h, ti * 512:(ti + 1) * 512],
                            r1[:, :], r2[:, :], ALU.add))
                        v.drain()
                        nit += 1
                barrier()
            if t_q is not None:
                sem_st.dma(sy.dma_start(t_q[:, :, :], q_bf[:, :, :]))
                sy.wait_ge(sem_st.h, sem_st.n)
            if phase_limit <= 1:
                sem_st.dma(g.dma_start(yout[0:128, :], q_bf[:, 0, :]))
                sy.wait_ge(sem_st.h, sem_st.n)
                return nc, tap_out

            with nc.sbuf_tensor("wk_bf", [128, NDC, D], BF16) as wk_bf, \
                 nc.sbuf_tensor("xc_bf", [128, 2, NDC, 512], BF16) as xc_bf, \
                 nc.sbuf_tensor("ropek_s", [128, S], F32) as ropek_s, \
                 nc.sbuf_tensor("kst", [128, 2, 512], BF16) as kst, \
                 nc.sbuf_tensor("kr1", [64, 512], F32) as kr1, \
                 nc.sbuf_tensor("kr2", [64, 512], F32) as kr2, \
                 nc.psum_tensor("psk0", [128, 512], F32) as psk0, \
                 nc.psum_tensor("psk1", [128, 512], F32) as psk1:
                psk = (psk0, psk1)
                wkr = wkT.rearrange("(c p) o -> p c o", p=128)
                for i in range(4):
                    sem_g.dma(g.dma_start(wk_bf[:, 4 * i:4 * (i + 1), :],
                                          wkr[:, 4 * i:4 * (i + 1), :]))
                sem_s.dma(sy.dma_start(ropek_s[:, :], ropek))
                v.wait_ge(sem_s.h, sem_s.n)
                xcr = xcov.rearrange("(c p) n -> p c n", p=128)
                pe0, st0 = sem_pe.n, sem_st.n
                rpd = {}
                nit = 0
                for tau in range(NTAU):
                    xb = tau % 2
                    if tau >= 2:
                        g.wait_ge(sem_pe.h, pe0 + 16 * (tau - 1))
                    sem_g.dma(g.dma_start(xc_bf[:, xb, :, :],
                                          xcr[:, :, tau * 512:(tau + 1) * 512]))
                    t.wait_ge(sem_g.h, sem_g.n)
                    for h in range(H):
                        bank = psk[nit % 2]
                        kb = kst[:, nit % 2, :]
                        if nit >= 2:
                            t.wait_ge(sem_ve.h, rpd[nit - 2])
                        for dc in range(NDC):
                            mm = t.matmul(bank[:, :],
                                          wk_bf[:, dc, h * 128:(h + 1) * 128],
                                          xc_bf[:, xb, dc, :],
                                          start=(dc == 0), stop=(dc == NDC - 1))
                        sem_pe.inc(mm)
                        v.wait_ge(sem_pe.h, sem_pe.n)
                        if nit >= 2:
                            v.wait_ge(sem_st.h, st0 + 16 * (nit - 1))
                        co = ropek_s[0:64, tau * 512:(tau + 1) * 512]
                        si = ropek_s[64:128, tau * 512:(tau + 1) * 512]
                        v.tensor_tensor(kr1[:, :], bank[0:64, :], co, ALU.mult)
                        v.drain()
                        v.tensor_tensor(kr2[:, :], bank[64:128, :], si, ALU.mult)
                        v.drain()
                        v.tensor_tensor(kb[0:64, :], kr1[:, :], kr2[:, :],
                                        ALU.subtract)
                        v.drain()
                        v.tensor_tensor(kr1[:, :], bank[0:64, :], si, ALU.mult)
                        v.drain()
                        v.tensor_tensor(kr2[:, :], bank[64:128, :], co, ALU.mult)
                        v.drain()
                        rpd[nit] = sem_ve.inc(v.tensor_tensor(
                            kb[64:128, :], kr1[:, :], kr2[:, :], ALU.add))
                        v.drain()
                        sy.wait_ge(sem_ve.h, sem_ve.n)
                        sem_st.dma(sy.dma_start(
                            k_all[h * 128:(h + 1) * 128,
                                  tau * 512:(tau + 1) * 512], kb[:, :]))
                        nit += 1
                barrier()
            if t_k is not None:
                with nc.sbuf_tensor("ktap", [128, NDC, S], BF16) as ktap:
                    sem_s.dma(sy.dma_start(
                        ktap[:, :, :], k_all.rearrange("(c p) n -> p c n", p=128)))
                    sy.wait_ge(sem_s.h, sem_s.n)
                    sem_st.dma(sy.dma_start(t_k[:, :, :], ktap[:, :, :]))
                    sy.wait_ge(sem_st.h, sem_st.n)
                    barrier()

            with nc.sbuf_tensor("wv_bf", [128, NDC, D], BF16) as wv_bf, \
                 nc.sbuf_tensor("xc2_bf", [128, 2, NDC, 512], BF16) as xc2_bf, \
                 nc.sbuf_tensor("vst", [128, 2, D], BF16) as vst, \
                 nc.psum_tensor("psv0", [128, 512], F32) as psv0, \
                 nc.psum_tensor("psv1", [128, 512], F32) as psv1:
                psv = (psv0, psv1)
                wvr = wvT.rearrange("(c p) o -> p c o", p=128)
                for i in range(4):
                    sem_g.dma(g.dma_start(wv_bf[:, 4 * i:4 * (i + 1), :],
                                          wvr[:, 4 * i:4 * (i + 1), :]))
                xcr = xcov.rearrange("(c p) n -> p c n", p=128)
                pe0, st0 = sem_pe.n, sem_st.n
                scd = {}
                nit = 0
                for tau in range(NTAU):
                    xb = tau % 2
                    if tau >= 2:
                        g.wait_ge(sem_pe.h, pe0 + 16 * (tau - 1))
                    sem_g.dma(g.dma_start(xc2_bf[:, xb, :, :],
                                          xcr[:, :, tau * 512:(tau + 1) * 512]))
                    t.wait_ge(sem_g.h, sem_g.n)
                    for tb in range(4):
                        vb = vst[:, nit % 2, :]
                        for hg in range(4):
                            it4 = nit * 4 + hg
                            bank = psv[it4 % 2]
                            if it4 >= 2:
                                t.wait_ge(sem_sc.h, scd[it4 - 2])
                            for dc in range(NDC):
                                mm = t.matmul(
                                    bank[:, :],
                                    xc2_bf[:, xb, dc, tb * 128:(tb + 1) * 128],
                                    wv_bf[:, dc, hg * 512:(hg + 1) * 512],
                                    start=(dc == 0), stop=(dc == NDC - 1))
                            sem_pe.inc(mm)
                            s.wait_ge(sem_pe.h, sem_pe.n)
                            if nit >= 2 and hg == 0:
                                s.wait_ge(sem_st.h, st0 + 16 * (nit - 1))
                            scd[it4] = sem_sc.inc(
                                s.copy(vb[:, hg * 512:(hg + 1) * 512], bank[:, :]))
                        sy.wait_ge(sem_sc.h, sem_sc.n)
                        sem_st.dma(sy.dma_start(
                            v_all[(tau * 4 + tb) * 128:(tau * 4 + tb + 1) * 128, :],
                            vb[:, :]))
                        nit += 1
                barrier()
            if t_v is not None:
                with nc.sbuf_tensor("vtap", [128, NKB, D], BF16) as vtap:
                    sem_s.dma(sy.dma_start(
                        vtap[:, :, :], v_all.rearrange("(c p) n -> p c n", p=128)))
                    sy.wait_ge(sem_s.h, sem_s.n)
                    sem_st.dma(sy.dma_start(t_v[:, :, :], vtap[:, :, :]))
                    sy.wait_ge(sem_st.h, sem_st.n)
                    barrier()
            if phase_limit <= 2:
                sem_st.dma(g.dma_start(yout[0:128, :], q_bf[:, 0, :]))
                sy.wait_ge(sem_st.h, sem_st.n)
                return nc, tap_out

            _ph = ExitStack()
            k_h = _ph.enter_context(nc.sbuf_tensor("k_h", [128, 4, S], BF16))
            v_h = _ph.enter_context(nc.sbuf_tensor("v_h", [128, 2, NKB, 256], BF16))
            attnT = _ph.enter_context(nc.sbuf_tensor("attnT", [128, NKB, 128], BF16))
            maskS_s = _ph.enter_context(nc.sbuf_tensor("maskS_s", [128, 512], F32))
            maskT_s = _ph.enter_context(nc.sbuf_tensor("maskT_s", [128, 512], F32))
            subsC = _ph.enter_context(nc.sbuf_tensor("subsC", [128, 2, 8, 128], F32))
            linv = _ph.enter_context(nc.sbuf_tensor("linv", [1, 128], F32))
            bv = _ph.enter_context(nc.sbuf_tensor("bv", [128, 128], F32))
            super_p = _ph.enter_context(nc.psum_tensor("super_p", [128, 2048], F32))
            pctx = _ph.enter_context(nc.psum_tensor("pctx", [128, 128], F32))
            psmA = _ph.enter_context(nc.psum_tensor("psmA", [128, 512], F32))
            if True:
                sem_s.dma(sy.dma_start(maskS_s[:, :], maskS))
                sem_s.dma(sy.dma_start(maskT_s[:, :], maskT))
                vrd = v_all.rearrange("(b p) d -> p b d", p=128)
                pair_done = {}

                def fetch_pair(hp, buf):
                    for i in range(2):
                        hh = 2 * hp + i
                        sem_s.dma(sy.dma_start(k_h[:, 2 * buf + i, :],
                                               k_all[hh * 128:(hh + 1) * 128, :]))
                    sem_s.dma(sy.dma_start(
                        v_h[:, buf, :, :],
                        vrd[:, :, 2 * hp * 128:2 * hp * 128 + 256]))
                    pair_done[hp] = sem_s.n

                fetch_pair(0, 0)
                pe_after_pair = {}
                for hp in range(p2_pairs):
                    buf = hp % 2
                    if hp + 1 < H // 2:
                        if hp >= 1:
                            sy.wait_ge(sem_pe.h, pe_after_pair[hp - 1])
                        fetch_pair(hp + 1, (hp + 1) % 2)
                    for hi in range(2):
                        h = 2 * hp + hi
                        kh = k_h[:, 2 * buf + hi, :]
                        t.wait_ge(sem_s.h, pair_done[hp])
                        s.wait_ge(sem_s.h, pair_done[hp])
                        pe_prev_m = sem_pe.n
                        for m in range(NM):
                            nb = 4 * (m + 1)
                            chunks = []
                            st_ = 0
                            while st_ < nb:
                                chunks.append((st_, min(8, nb - st_)))
                                st_ += 8
                            ncks = len(chunks)
                            pe_psT = {}
                            sc_cp = {}
                            ve_tt = {}
                            sc_ex = {}

                            def emit_psT(c):
                                cst, clen = chunks[c]
                                base = (c % 2) * 1024
                                if c >= 2:
                                    t.wait_ge(sem_sc.h, sc_cp[c - 2])
                                for j in range(clen):
                                    b = cst + j
                                    mm = t.matmul(
                                        super_p[:, base + j * 128:base + (j + 1) * 128],
                                        kh[:, b * 128:(b + 1) * 128],
                                        q_bf[:, h, m * 128:(m + 1) * 128],
                                        start=True, stop=True)
                                pe_psT[c] = sem_pe.inc(mm)

                            def emit_mid(c):
                                cst, clen = chunks[c]
                                base = (c % 2) * 1024
                                half = c % 2
                                s.wait_ge(sem_pe.h, pe_psT[c])
                                sc_cp[c] = sem_sc.inc(s.copy(
                                    subsC[:, half, 0:clen, :],
                                    super_p[:, base:base + clen * 128]))
                                v.wait_ge(sem_sc.h, sc_cp[c])
                                for j4 in range(0, clen, 4):
                                    gabs = (cst + j4) // 4
                                    msk = maskT_s if gabs == m else maskS_s
                                    ii = v.tensor_tensor(
                                        subsC[:, half, j4:j4 + 4, :],
                                        subsC[:, half, j4:j4 + 4, :],
                                        msk.rearrange("p (a b) -> p a b", a=4),
                                        ALU.add)
                                ve_tt[c] = sem_ve.inc(ii)
                                if c == 0:
                                    s.wait_ge(sem_pe.h, pe_prev_m)
                                s.wait_ge(sem_ve.h, ve_tt[c])
                                sc_ex[c] = sem_sc.inc(s.activation(
                                    attnT[:, cst:cst + clen, :],
                                    subsC[:, half, 0:clen, :], AF.Exp))

                            def emit_ctx(c):
                                cst, clen = chunks[c]
                                t.wait_ge(sem_sc.h, sc_ex[c])
                                for j in range(clen):
                                    b = cst + j
                                    t.matmul(pctx[:, :],
                                             v_h[:, buf, b, hi * 128:(hi + 1) * 128],
                                             attnT[:, b, :],
                                             start=(b == 0), stop=(b == nb - 1))
                                for j in range(clen):
                                    b = cst + j
                                    mm = t.matmul(psmA[0:1, 0:128], onesc_b[:, :],
                                                  attnT[:, b, :],
                                                  start=(b == 0), stop=(b == nb - 1))
                                sem_pe.inc(mm)

                            emit_psT(0)
                            for c in range(ncks):
                                if c + 1 < ncks:
                                    emit_mid(c)
                                    emit_psT(c + 1)
                                    emit_ctx(c)
                                else:
                                    emit_mid(c)
                                    emit_ctx(c)
                            v.wait_ge(sem_pe.h, sem_pe.n)
                            sem_ve.inc(v.reciprocal(linv[:, :], psmA[0:1, 0:128]))
                            t.wait_ge(sem_ve.h, sem_ve.n)
                            sem_pe.inc(t.matmul(psmA[:, 128:256], ones1_s[:, :],
                                                linv[:, :], start=True, stop=True))
                            s.wait_ge(sem_pe.h, sem_pe.n)
                            sem_sc.inc(s.copy(bv[:, :], psmA[:, 128:256]))
                            v.wait_ge(sem_sc.h, sem_sc.n)
                            sem_ve.inc(v.tensor_tensor(
                                ctx_bf[:, h, m * 128:(m + 1) * 128],
                                pctx[:, :], bv[:, :], ALU.mult))
                            t.wait_ge(sem_ve.h, sem_ve.n)
                            pe_prev_m = sem_pe.n
                    pe_after_pair[hp] = sem_pe.n
                barrier()
            if t_ctx is not None:
                sem_st.dma(sy.dma_start(t_ctx[:, :, :], ctx_bf[:, :, :]))
                sy.wait_ge(sem_st.h, sem_st.n)
            if phase_limit <= 3:
                sem_st.dma(g.dma_start(yout[0:128, :], q_bf[:, 0, :]))
                sy.wait_ge(sem_st.h, sem_st.n)
                return nc, tap_out

         _ph.close()
         q_g.__exit__(None, None, None)
         sf = sbr("sf", [128, NDC, NOWN])

         with nc.sbuf_tensor("wo_bf", [128, NDC, D], BF16) as wo_bf, \
              nc.sbuf_tensor("xst", [128, 2, 512], F32) as xst, \
              nc.sbuf_tensor("lnb", [128, 2, 512], BF16) as lnb, \
              nc.sbuf_tensor("lnq", [128, 2, 512], BF16) as lnq, \
              nc.psum_tensor("pw0", [128, 512], F32) as pw0, \
              nc.psum_tensor("pw1", [128, 512], F32) as pw1, \
              nc.psum_tensor("psA", [1, 512], F32) as psA, \
              nc.psum_tensor("psB", [1, 512], F32) as psB, \
              nc.psum_tensor("psBC", [128, 512], F32) as psBC:
                pw = (pw0, pw1)
                wor = woT.rearrange("(c p) o -> p c o", p=128)
                for i in range(4):
                    sem_g.dma(g.dma_start(wo_bf[:, 4 * i:4 * (i + 1), :],
                                          wor[:, 4 * i:4 * (i + 1), :]))
                t.wait_ge(sem_g.h, sem_g.n)
                xo_r = x_own.rearrange("(c p) n -> p c n", p=128)
                add_done = {}
                nit = 0
                for dc in range(NDC):
                    for ti in range(2):
                        if nit >= 2:
                            s.wait_ge(sem_ve.h, add_done[nit - 2])
                            t.wait_ge(sem_ve.h, add_done[nit - 2])
                        sem_a.dma(s.dma_start(xst[:, nit % 2, :],
                                              xo_r[:, dc, ti * 512:(ti + 1) * 512]))
                        bank = pw[nit % 2]
                        for hc in range(NDC):
                            mm = t.matmul(bank[:, :],
                                          wo_bf[:, hc, dc * 128:(dc + 1) * 128],
                                          ctx_bf[:, hc, ti * 512:(ti + 1) * 512],
                                          start=(hc == 0), stop=(hc == NDC - 1))
                        sem_pe.inc(mm)
                        v.wait_ge(sem_pe.h, sem_pe.n)
                        v.wait_ge(sem_a.h, sem_a.n)
                        add_done[nit] = sem_ve.inc(v.tensor_tensor(
                            sf[:, dc, ti * 512:(ti + 1) * 512],
                            bank[:, :], xst[:, nit % 2, :], ALU.add))
                        nit += 1
                for e in ENGS:
                    e.wait_ge(sem_ve.h, sem_ve.n)
                emit_ln(sf, lnb, lnq, psA, psB, psBC)
                barrier()
         if t_y1 is not None:
                sem_st.dma(sy.dma_start(t_y1[:, :, :], sf[:, :, :]))
                sy.wait_ge(sem_st.h, sem_st.n)
         if phase_limit <= 4:
                sem_st.dma(sy.dma_start(yout[0:128, :], sf[:, 0, 0:1024]))
                sy.wait_ge(sem_st.h, sem_st.n)
                return nc, tap_out

         ctx_g.__exit__(None, None, None)

         _ph = ExitStack()
         y1bf = _ph.enter_context(nc.sbuf_tensor("y1bf", [128, NDC, NOWN], BF16))
         rw_s = _ph.enter_context(nc.sbuf_tensor("rw_s", [128, NDC, 8], F32))
         lrow = _ph.enter_context(nc.sbuf_tensor("lrow", [128, 8], F32))
         m1r = _ph.enter_context(nc.sbuf_tensor("m1r", [128, 1], F32))
         m2r = _ph.enter_context(nc.sbuf_tensor("m2r", [128, 1], F32))
         w1r = _ph.enter_context(nc.sbuf_tensor("w1r", [128, 1], F32))
         w2r = _ph.enter_context(nc.sbuf_tensor("w2r", [128, 1], F32))
         eq1 = _ph.enter_context(nc.sbuf_tensor("eq1", [128, 8], F32))
         eq2 = _ph.enter_context(nc.sbuf_tensor("eq2", [128, 8], F32))
         lmask = _ph.enter_context(nc.sbuf_tensor("lmask", [128, 8], F32))
         wtm = _ph.enter_context(nc.sbuf_tensor("wtm", [128, NM, 8], F32))
         wfm = _ph.enter_context(nc.sbuf_tensor("wfm", [8, NOWN], F32))
         wbc = _ph.enter_context(nc.sbuf_tensor("wbc", [128, E, NOWN], F32))
         w12t = _ph.enter_context(nc.sbuf_tensor("w12t", [128, 2, 2, NDC, 128], BF16))
         w3t = _ph.enter_context(nc.sbuf_tensor("w3t", [128, 2, NFC, 128], BF16))
         gsb = _ph.enter_context(nc.sbuf_tensor("gsb", [128, 2, 512], F32))
         hfs = _ph.enter_context(nc.sbuf_tensor("hfs", [128, 2, 512], F32))
         hbf = _ph.enter_context(nc.sbuf_tensor("hbf", [128, NFC, NOWN], BF16))
         pg0 = _ph.enter_context(nc.psum_tensor("pg0", [128, 512], F32))
         pg1 = _ph.enter_context(nc.psum_tensor("pg1", [128, 512], F32))
         pv0 = _ph.enter_context(nc.psum_tensor("pv0", [128, 512], F32))
         pv1 = _ph.enter_context(nc.psum_tensor("pv1", [128, 512], F32))
         po0 = _ph.enter_context(nc.psum_tensor("po0", [128, 512], F32))
         po1 = _ph.enter_context(nc.psum_tensor("po1", [128, 512], F32))
         psm4 = _ph.enter_context(nc.psum_tensor("psm4", [128, 256], F32))
         if True:
                pg = (pg0, pg1)
                pv = (pv0, pv1)
                po = (po0, po1)
                for dc in range(NDC):
                    ii = g.tensor_copy(y1bf[:, dc, :], sf[:, dc, :])
                sem_po.inc(ii)
                cast_done = sem_po.n
                sem_s.dma(sy.dma_start(
                    rw_s[:, :, :], rwT.rearrange("(c p) e -> p c e", p=128)))
                t.wait_ge(sem_s.h, sem_s.n)
                for blk in range(NM):
                    reg = psm4[:, 0:8]
                    if blk >= 1:
                        t.wait_ge(sem_ve.h, sem_ve.n)
                    for dc in range(NDC):
                        mm = t.matmul(reg, sf[:, dc, blk * 128:(blk + 1) * 128],
                                      rw_s[:, dc, :],
                                      start=(dc == 0), stop=(dc == NDC - 1))
                    sem_pe.inc(mm)
                    v.wait_ge(sem_pe.h, sem_pe.n)
                    v.tensor_copy(lrow[:, :], reg)
                    v.drain()
                    v.reduce_max(m1r[:, :], lrow[:, :], axis=AX.X)
                    v.drain()
                    v.tensor_single_scalar(eq1[:, :], lrow[:, :], m1r[:, :],
                                           ALU.is_equal)
                    v.drain()
                    v.scalar_tensor_tensor(lmask[:, :], eq1[:, :], NEG,
                                           lrow[:, :], ALU.mult, ALU.add)
                    v.drain()
                    v.reduce_max(m2r[:, :], lmask[:, :], axis=AX.X)
                    v.drain()
                    v.tensor_single_scalar(eq2[:, :], lmask[:, :], m2r[:, :],
                                           ALU.is_equal)
                    v.drain()
                    sem_ve.inc(v.tensor_tensor(w2r[:, :], m2r[:, :], m1r[:, :],
                                               ALU.subtract))
                    s.wait_ge(sem_ve.h, sem_ve.n)
                    sem_sc.inc(s.activation(w2r[:, :], w2r[:, :], AF.Sigmoid))
                    v.wait_ge(sem_sc.h, sem_sc.n)
                    v.tensor_scalar(w1r[:, :], w2r[:, :], -1.0, 1.0,
                                    ALU.mult, ALU.add)
                    v.drain()
                    v.tensor_single_scalar(eq1[:, :], eq1[:, :], w1r[:, :],
                                           ALU.mult)
                    v.drain()
                    sem_ve.inc(v.scalar_tensor_tensor(
                        wtm[:, blk, :], eq2[:, :], w2r[:, :], eq1[:, :],
                        ALU.mult, ALU.add))
                tr_done = {}
                for blk in range(NM):
                    reg8 = psm4[0:8, 128:256]
                    if blk >= 1:
                        t.wait_ge(sem_ve.h, tr_done[blk - 1])
                    else:
                        t.wait_ge(sem_ve.h, sem_ve.n)
                    sem_pe.inc(t.matmul(reg8, wtm[:, blk, :], i128_s[:, :],
                                        start=True, stop=True))
                    v.wait_ge(sem_pe.h, sem_pe.n)
                    tr_done[blk] = sem_ve.inc(v.tensor_copy(
                        wfm[:, blk * 128:(blk + 1) * 128], reg8))
                bc_done = {}
                for e_ in range(E):
                    for ti in range(2):
                        reg = po[ti]
                        if e_ >= 1:
                            t.wait_ge(sem_sc.h, bc_done[(e_ - 1, ti)])
                        if e_ == 0 and ti == 0:
                            t.wait_ge(sem_ve.h, sem_ve.n)
                        sem_pe.inc(t.matmul(reg[:, :],
                                            sel_s[:, e_ * 128:(e_ + 1) * 128],
                                            wfm[:, ti * 512:(ti + 1) * 512],
                                            start=True, stop=True))
                        s.wait_ge(sem_pe.h, sem_pe.n)
                        bc_done[(e_, ti)] = sem_sc.inc(
                            s.copy(wbc[:, e_, ti * 512:(ti + 1) * 512], reg[:, :]))
                for e in ENGS:
                    e.wait_ge(sem_sc.h, sem_sc.n)
                if t_w is not None:
                    sem_st.dma(sy.dma_start(t_w[:, :, :], wtm[:, :, :]))
                    sy.wait_ge(sem_st.h, sem_st.n)
                w1r_ = w1.rearrange("e (c p) f -> e p c f", p=128)
                w2r_ = w2.rearrange("e (c p) f -> e p c f", p=128)
                w3r_ = w3.rearrange("e (c p) o -> e p c o", p=128)

                def fetch_w12(e_, fc, bf):
                    sem_g.dma(g.dma_start(
                        w12t[:, bf, 0, :, :],
                        w1r_[e_, :, :, fc * 128:(fc + 1) * 128]))
                    sem_g.dma(g.dma_start(
                        w12t[:, bf, 1, :, :],
                        w2r_[e_, :, :, fc * 128:(fc + 1) * 128]))
                    return sem_g.n

                def fetch_w3(e_, dc, bf):
                    sem_g.dma(g.dma_start(
                        w3t[:, bf, :, :],
                        w3r_[e_, :, :, dc * 128:(dc + 1) * 128]))
                    return sem_g.n

                t.wait_ge(sem_po.h, cast_done)
                sc_g, ve_g, po_g, ve_o = {}, {}, {}, {}
                pe_w12_done, pe_w3_done, pe_w3_mark = {}, {}, {}
                w12_ready = {(0, 0): fetch_w12(0, 0, 0)}
                it = 0
                for e_ in range(E):
                    for fc in range(NFC):
                        bf = it % 2
                        nxt = (e_, fc + 1) if fc + 1 < NFC else \
                            ((e_ + 1, 0) if e_ + 1 < E else None)
                        if nxt is not None:
                            if it >= 1:
                                g.wait_ge(sem_pe.h, pe_w12_done[it - 1])
                            w12_ready[nxt] = fetch_w12(nxt[0], nxt[1],
                                                       (it + 1) % 2)
                        t.wait_ge(sem_g.h, w12_ready[(e_, fc)])
                        for ti in range(2):
                            it2 = it * 2 + ti
                            bg = pg[it2 % 2]
                            bvv = pv[it2 % 2]
                            if it2 >= 2:
                                t.wait_ge(sem_sc.h, sc_g[it2 - 2])
                                t.wait_ge(sem_ve.h, ve_g[it2 - 2])
                            for dc in range(NDC):
                                t.matmul(bg[:, :], w12t[:, bf, 0, dc, :],
                                         y1bf[:, dc, ti * 512:(ti + 1) * 512],
                                         start=(dc == 0), stop=(dc == NDC - 1))
                            for dc in range(NDC):
                                mm = t.matmul(bvv[:, :], w12t[:, bf, 1, dc, :],
                                              y1bf[:, dc, ti * 512:(ti + 1) * 512],
                                              start=(dc == 0),
                                              stop=(dc == NDC - 1))
                            sem_pe.inc(mm)
                            s.wait_ge(sem_pe.h, sem_pe.n)
                            if it2 >= 2:
                                s.wait_ge(sem_ve.h, ve_g[it2 - 2])
                            sc_g[it2] = sem_sc.inc(s.activation(
                                gsb[:, it2 % 2, :], bg[:, :], AF.Silu))
                            v.wait_ge(sem_sc.h, sem_sc.n)
                            if it2 >= 2:
                                v.wait_ge(sem_po.h, po_g[it2 - 2])
                            ve_g[it2] = sem_ve.inc(v.tensor_tensor(
                                hfs[:, it2 % 2, :], bvv[:, :],
                                gsb[:, it2 % 2, :], ALU.mult))
                            g.wait_ge(sem_ve.h, sem_ve.n)
                            if e_ > 0:
                                g.wait_ge(sem_pe.h, pe_w3_done[e_ - 1])
                            po_g[it2] = sem_po.inc(g.tensor_tensor(
                                hbf[:, fc, ti * 512:(ti + 1) * 512],
                                hfs[:, it2 % 2, :],
                                wbc[:, e_, ti * 512:(ti + 1) * 512], ALU.mult))
                        pe_w12_done[it] = sem_pe.n
                        it += 1
                    w3_ready = {0: fetch_w3(e_, 0, 0)}
                    t.wait_ge(sem_po.h, sem_po.n)
                    for dc in range(NDC):
                        bf3 = dc % 2
                        if dc + 1 < NDC:
                            if dc >= 1:
                                g.wait_ge(sem_pe.h, pe_w3_mark[dc - 1])
                            w3_ready[dc + 1] = fetch_w3(e_, dc + 1, (dc + 1) % 2)
                        t.wait_ge(sem_g.h, w3_ready[dc])
                        for ti in range(2):
                            it3 = dc * 2 + ti
                            bo = po[it3 % 2]
                            if it3 >= 2:
                                t.wait_ge(sem_ve.h, ve_o[it3 - 2])
                            for fc in range(NFC):
                                mm = t.matmul(
                                    bo[:, :], w3t[:, bf3, fc, :],
                                    hbf[:, fc, ti * 512:(ti + 1) * 512],
                                    start=(fc == 0), stop=(fc == NFC - 1))
                            sem_pe.inc(mm)
                            if ti == 1:
                                pe_w3_mark[dc] = sem_pe.n
                            v.wait_ge(sem_pe.h, sem_pe.n)
                            ve_o[it3] = sem_ve.inc(v.tensor_tensor(
                                sf[:, dc, ti * 512:(ti + 1) * 512], bo[:, :],
                                sf[:, dc, ti * 512:(ti + 1) * 512], ALU.add))
                    pe_w3_done[e_] = sem_pe.n
                for e in ENGS:
                    e.wait_ge(sem_ve.h, sem_ve.n)
                barrier()

                with nc.sbuf_tensor("ln2b", [128, 2, 512], BF16) as ln2b, \
                     nc.sbuf_tensor("ln2q", [128, 2, 512], BF16) as ln2q:
                    emit_ln(sf, ln2b, ln2q, pg0, pg1, po0)
                    barrier()
                    for dc in range(NDC):
                        sem_st.dma(sy.dma_start(
                            yout[dc * 128:(dc + 1) * 128, :], sf[:, dc, :]))
                    sy.wait_ge(sem_st.h, sem_st.n)
         _ph.close()
    return nc, tap_out



def _rope_tables(positions):
    inv = THETA ** (-np.arange(0, DK, 2, dtype=np.float64) / DK)
    ang = positions[None, :].astype(np.float64) * inv[:, None]
    return np.cos(ang).astype(np.float32), np.sin(ang).astype(np.float32)


def _perm_rope():
    p = np.zeros(D, dtype=np.int64)
    for h in range(H):
        base = h * DK
        p[base:base + 64] = base + 2 * np.arange(64)
        p[base + 64:base + 128] = base + 2 * np.arange(64) + 1
    return p


def host_inputs(L0_attn_x, L0_attn_Wq, L0_attn_Wk, L0_attn_Wv, L0_attn_Wo,
                L0_ffn_router_w, L0_ffn_w1, L0_ffn_w2, L0_ffn_w3):
    xT = np.ascontiguousarray(
        np.asarray(L0_attn_x, np.float32).reshape(B * S, D).T)
    perm = _perm_rope()
    wqT = np.ascontiguousarray(np.asarray(L0_attn_Wq, np.float32).T[:, perm])
    wkT = np.ascontiguousarray(np.asarray(L0_attn_Wk, np.float32).T[:, perm])
    wvT = np.ascontiguousarray(np.asarray(L0_attn_Wv, np.float32).T)
    woT = np.ascontiguousarray(np.asarray(L0_attn_Wo, np.float32).T)
    rwT = np.ascontiguousarray(np.asarray(L0_ffn_router_w, np.float32).T)
    w1 = np.ascontiguousarray(np.asarray(L0_ffn_w1, np.float32))
    w2 = np.ascontiguousarray(np.asarray(L0_ffn_w2, np.float32))
    w3 = np.ascontiguousarray(np.asarray(L0_ffn_w3, np.float32))

    i128 = np.eye(128, dtype=np.float32)
    ones1 = np.ones((1, 128), np.float32)
    onesc = np.ones((128, 1), np.float32)
    sel = np.zeros((8, 1024), np.float32)
    for e in range(8):
        sel[e, e * 128:(e + 1) * 128] = 1.0

    cosk, sink = _rope_tables(np.arange(S))
    ropek = np.ascontiguousarray(np.concatenate([cosk, sink], axis=0))

    in_maps = []
    for c in range(NCORE):
        sq = c // 4
        r = c % 4
        blocks = [4 * m + r for m in range(NM)]
        own_pos = np.concatenate(
            [np.arange(b * 128, (b + 1) * 128) for b in blocks])
        cosq, sinq = _rope_tables(own_pos)
        ropeq = np.ascontiguousarray(
            np.concatenate([cosq * ISQ, sinq * ISQ], axis=0))
        maskS_c = np.full((128, 512), -SHIFT, np.float32)
        kk = np.arange(128)[:, None]
        ii = np.arange(128)[None, :]
        mT = np.zeros((128, 512), np.float32)
        for j in range(4):
            mT[:, j * 128:(j + 1) * 128] = np.where(
                (j - r) * 128 + kk <= ii, -SHIFT, NEG)
        xcov_c = np.ascontiguousarray(xT[:, sq * S:(sq + 1) * S])
        x_own_c = np.ascontiguousarray(xcov_c[:, own_pos])
        in_maps.append(dict(
            xcov=xcov_c, x_own=x_own_c, ropeq=ropeq, ropek=ropek,
            maskS=maskS_c, maskT=np.ascontiguousarray(mT),
            i128=i128, ones1=ones1, onesc=onesc, sel=sel,
            wqT=wqT, wkT=wkT, wvT=wvT, woT=woT, rwT=rwT,
            w1=w1, w2=w2, w3=w3))
    return in_maps


def assemble_output(results):
    out = np.zeros((B, S, D), np.float32)
    for c in range(NCORE):
        sq = c // 4
        r = c % 4
        y = results[c]["yout"]
        for m in range(NM):
            b = 4 * m + r
            out[sq, b * 128:(b + 1) * 128, :] = y[:, m * 128:(m + 1) * 128].T
    return out


_NC_CACHE = {}


def kernel(**inputs):
    in_maps = host_inputs(**inputs)
    if "full" not in _NC_CACHE:
        _NC_CACHE["full"] = build_nc()
    nc, _ = _NC_CACHE["full"]
    res = run_bass_kernel_spmd(nc, in_maps, core_ids=list(range(NCORE)))
    return assemble_output(res.results)
